# version 08
# baseline (speedup 1.0000x reference)
import math
import numpy as np
import ml_dtypes
import concourse.bass as bass
import concourse.mybir as mybir
from concourse.bass_utils import run_bass_kernel_spmd

F32 = mybir.dt.float32
BF16 = mybir.dt.bfloat16
AF = mybir.ActivationFunctionType
ALU = mybir.AluOpType
AX = mybir.AxisListType

D = 2048
T = 2048
NTT = 16
H = 16
DH = 128
FF = 8192
SFULL = 8192
MC = 64
NBH = 2
GRP = 3
NG = 22
JC = 64
DNG = 16
SCALE = 1.0 / math.sqrt(128.0)
EPS = 1e-5
RSFLOOR = 1.2e-38

GROUPS = [list(range(g * GRP, min((g + 1) * GRP, MC))) for g in range(NG)]
assert sum(len(g) for g in GROUPS) == MC and len(GROUPS[-1]) == 1

R1 = 81888
R2 = 147424
O_XT = (0, 8192)
O_WQB = (16384, 20480)
O_WKB = (24576, 28672)
O_WVB = (32768, 40960)
O_KSTG = (49152, 53248)
O_QSTG = (57344, 61440)
O_VSTG = (65536, 66048)
O_SQSCR = 66560
O_KSB = (0, 16384)
O_VSB = (32768, 49280)
O_QSB = (76032, 78080)
O_PSL = (69888, 72960)
O_H2T = 0
O_GG = (32768, 34816, 36864, 38912)
O_LN2SCR = 32768
O_WGB = (40960, 45056)
O_WUB = (49152, 53248)
O_WDG = (57344, 61440, 65536, 69632)
O_STMP = 73728
O_H2BF = (73728, 77824)

_NC_CACHE = {}
ATT_ONLY = False
BASE_ONLY = False
TAP_EARLY = False


def _sb_at(nc, name, shape, dtype, off):
    return nc.alloc_sbuf_tensor_at(name, shape, dtype, offset=off)


def build_program(debug=False):
    nc = bass.Bass("TRN2", target_bir_lowering=False)

    x_in = nc.dram_tensor("x", [T, D], F32, kind="ExternalInput").ap()
    wq_t = nc.dram_tensor("wq_t", [16, 128, 2048], F32, kind="ExternalInput").ap()
    wk_t = nc.dram_tensor("wk_t", [16, 128, 2048], F32, kind="ExternalInput").ap()
    wv_t = nc.dram_tensor("wv_t", [8, 128, 4096], F32, kind="ExternalInput").ap()
    bq_h = nc.dram_tensor("bq_h", [128, 16], F32, kind="ExternalInput").ap()
    bk_h = nc.dram_tensor("bk_h", [128, 16], F32, kind="ExternalInput").ap()
    bv_h = nc.dram_tensor("bv_h", [1, 2048], F32, kind="ExternalInput").ap()
    wg_t = nc.dram_tensor("wg_t", [64, 128, 2048], F32, kind="ExternalInput").ap()
    wu_t = nc.dram_tensor("wu_t", [64, 128, 2048], F32, kind="ExternalInput").ap()
    wd_t = nc.dram_tensor("wd_t", [64, 128, 2048], F32, kind="ExternalInput").ap()
    out = nc.dram_tensor("out", [T, D], F32, kind="ExternalOutput").ap()

    wvb_d = nc.dram_tensor("wvb_d", [8, 128, 4096], BF16, kind="Internal").ap()
    wqb_d = nc.dram_tensor("wqb_d", [16, 128, 2048], BF16, kind="Internal").ap()
    qt_sp = nc.dram_tensor("qt_sp", [16, 128, 2048], BF16, kind="Internal").ap()
    kc_in = nc.dram_tensor("kc_in", [2048, 2048], BF16, kind="Internal").ap()
    kc_out = nc.dram_tensor("kc_out", [8192, 2048], BF16, kind="Internal").ap()
    vc_in = nc.dram_tensor("vc_in", [2048, 2064], BF16, kind="Internal").ap()
    vc_out = nc.dram_tensor("vc_out", [8192, 2064], BF16, kind="Internal").ap()
    st_in = nc.dram_tensor("st_in", [1, 16], F32, kind="Internal").ap()
    st_out = nc.dram_tensor("st_out", [4, 16], F32, kind="Internal").ap()
    if debug:
        dbg_xnew = nc.dram_tensor("dbg_xnew", [T, D], F32, kind="ExternalOutput").ap()
        dbg_ast = nc.dram_tensor("dbg_ast", [1, 16], F32, kind="ExternalOutput").ap()
        dbg_qt = nc.dram_tensor("dbg_qt", [16, 128, 2048], BF16, kind="ExternalOutput").ap()
        dbg_kc = nc.dram_tensor("dbg_kc", [8192, 2048], BF16, kind="ExternalOutput").ap()
        dbg_vc = nc.dram_tensor("dbg_vc", [8192, 2064], BF16, kind="ExternalOutput").ap()
        dbg_st = nc.dram_tensor("dbg_st", [4, 16], F32, kind="ExternalOutput").ap()

    ident = nc.alloc_sbuf_tensor("ident", [128, 128], BF16)
    ones_c = nc.alloc_sbuf_tensor("ones_c", [128, 1], BF16)
    ones_r = nc.alloc_sbuf_tensor("ones_r", [1, 128], BF16)
    bq_sb = nc.alloc_sbuf_tensor("bq_sb", [128, 16], F32)
    bk_sb = nc.alloc_sbuf_tensor("bk_sb", [128, 16], F32)
    bv_bf = nc.alloc_sbuf_tensor("bv_bf", [1, 2048], BF16)
    rstd1 = nc.alloc_sbuf_tensor("rstd1", [128, 16], F32)
    nmr1 = nc.alloc_sbuf_tensor("nmr1", [128, 16], F32)
    rstd1p = nc.alloc_sbuf_tensor("rstd1p", [128, 16], F32)
    rstd2 = nc.alloc_sbuf_tensor("rstd2", [128, 8], F32)
    nmr2 = nc.alloc_sbuf_tensor("nmr2", [128, 8], F32)
    work1 = nc.alloc_sbuf_tensor("work1", [128, 16 * 6], F32)
    work2 = nc.alloc_sbuf_tensor("work2", [128, 8 * 6], F32)
    ast = nc.alloc_sbuf_tensor("ast", [1, 16], F32)
    bst = nc.alloc_sbuf_tensor("bst", [1, 16], F32)
    stld = nc.alloc_sbuf_tensor("stld", [1, 64], F32)
    bmax = nc.alloc_sbuf_tensor("bmax", [1, 16], F32)
    ncw = nc.alloc_sbuf_tensor("ncw", [1, 64], F32)
    ncbf = nc.alloc_sbuf_tensor("ncbf", [1, 16], BF16)
    negc = nc.alloc_sbuf_tensor("negc", [128, 16], F32)
    ew = nc.alloc_sbuf_tensor("ew", [128, 8], F32)
    vspc = nc.alloc_sbuf_tensor("vspc", [128, 2], F32)
    dbgdump = nc.alloc_sbuf_tensor("dbgdump", [128, 128], F32) if (ATT_ONLY or BASE_ONLY or True) else None
    epsc = nc.alloc_sbuf_tensor("epsc", [128, 1], F32)

    hT = _sb_at(nc, "hT", [128, 16 * 2048], BF16, R1)
    xnew = _sb_at(nc, "xnew", [128, 8 * 2048], F32, R1)
    xt = [_sb_at(nc, f"xt{i}", [128, 2048], F32, R2 + O_XT[i]) for i in range(2)]
    wqb = [_sb_at(nc, f"wqb{i}", [128, 2048], BF16, R2 + O_WQB[i]) for i in range(2)]
    wkb = [_sb_at(nc, f"wkb{i}", [128, 2048], BF16, R2 + O_WKB[i]) for i in range(2)]
    wvb = [_sb_at(nc, f"wvb{i}", [128, 4096], BF16, R2 + O_WVB[i]) for i in range(2)]
    kstg = [_sb_at(nc, f"kstg{i}", [128, 2048], BF16, R2 + O_KSTG[i]) for i in range(2)]
    qstg = [_sb_at(nc, f"qstg{i}", [128, 2048], BF16, R2 + O_QSTG[i]) for i in range(2)]
    hbf = kstg
    vstg = [_sb_at(nc, f"vstg{i}", [128, 256], BF16, R2 + O_VSTG[i]) for i in range(2)]
    vstgb = _sb_at(nc, "vstgb", [128, 16, 2, 128], BF16, R2 + 70656)
    sqscr = _sb_at(nc, "sqscr", [128, 2048], BF16, R2 + O_SQSCR)
    ksb = [_sb_at(nc, f"ksb{i}", [128, 8192], BF16, R2 + O_KSB[i]) for i in range(2)]
    vsb = [_sb_at(nc, f"vsb{i}", [128, 64, 129], BF16, R2 + O_VSB[i]) for i in range(2)]
    qsb = [_sb_at(nc, f"qsb{i}", [128, 1024], BF16, R2 + O_QSB[i]) for i in range(2)]
    psl = [_sb_at(nc, f"psl{i}", [128, 1536], BF16, R2 + O_PSL[i]) for i in range(2)]
    h2T = _sb_at(nc, "h2T", [128, 16 * 1024], BF16, R2 + O_H2T)
    gg = [_sb_at(nc, f"gg{i}", [128, 1024], BF16, R2 + O_GG[i]) for i in range(4)]
    ln2scr = _sb_at(nc, "ln2scr", [128, 2048], BF16, R2 + O_LN2SCR)
    wgb = [_sb_at(nc, f"wgb{i}", [128, 2048], BF16, R2 + O_WGB[i]) for i in range(2)]
    wub = [_sb_at(nc, f"wub{i}", [128, 2048], BF16, R2 + O_WUB[i]) for i in range(2)]
    wdg = [_sb_at(nc, f"wdg{i}", [128, 2048], BF16, R2 + O_WDG[i]) for i in range(4)]
    stmp = _sb_at(nc, "stmp", [128, 1024], F32, R2 + O_STMP)
    h2bf = [_sb_at(nc, f"h2bf{i}", [128, 2048], BF16, R2 + O_H2BF[i]) for i in range(2)]

    pa = nc.alloc_psum_tensor("pa", [128, 4096], F32)
    pab = pa.bitcast(BF16)
    AVO = 3072
    AV_QS = [AVO + 0, AVO + 129, AVO + 512, AVO + 641]

    semnames = ["sd_x", "sa_h", "s_tr", "sv_t", "s_init", "s_ld1", "sa_r1", "s_ld2",
                "sd_w0", "sd_w1", "s_pmm", "sa_ev", "sa_sq", "s_stp", "s_std",
                "sd_ksp0", "sd_ksp1", "sd_qsp0", "sd_qsp1", "sd_sti",
                "sa_vev", "sd_vsp0", "sd_vsp1", "s_cc", "sa_xn", "s_nc",
                "sd_k0", "sd_k1", "sd_v0", "sd_v1", "sd_q0", "sd_q1",
                "s_vrdy", "s_sc", "s_exp", "s_av", "s_ep",
                "sd_wg0", "sd_wg1", "sd_wu0", "sd_wu1", "sd_wd", "s_gu", "sa_sil",
                "s_gmu", "s_dnm", "s_dnv", "sd_out", "sd_dbg", "s_pcv", "s_pcq", "sa_sqa"]
    sems = {n: nc.alloc_semaphore(n) for n in semnames}
    S = type("S", (), sems)
    SD_W = [S.sd_w0, S.sd_w1]
    SD_KSP = [S.sd_ksp0, S.sd_ksp1]
    SD_QSP = [S.sd_qsp0, S.sd_qsp1]
    SD_VSP = [S.sd_vsp0, S.sd_vsp1]
    SD_K = [S.sd_k0, S.sd_k1]
    SD_V = [S.sd_v0, S.sd_v1]
    SD_Q = [S.sd_q0, S.sd_q1]
    SD_WG = [S.sd_wg0, S.sd_wg1]
    SD_WU = [S.sd_wu0, S.sd_wu1]

    def gg_idx(hf, h, nb, g):
        return (((hf * 16 + h) * 2) + nb) * NG + g

    def kd_idx(hf, grp, tt, ds):
        return hf * 512 + grp * 32 + tt * 4 + ds

    dbg_sync_extra = []

    def em_sync(sync):
        sync.dma_start(out=bq_sb[:, :], in_=bq_h).then_inc(S.sd_sti, 16)
        sync.dma_start(out=bk_sb[:, :], in_=bk_h).then_inc(S.sd_sti, 16)
        for tt in range(NTT):
            if tt >= 2:
                sync.wait_ge(S.sa_h, tt - 1)
            sync.dma_start(out=xt[tt % 2][:, :],
                           in_=x_in[tt * 128:(tt + 1) * 128, :]).then_inc(S.sd_x, 16)
        for oc in range(16):
            sync.wait_ge(S.sa_ev, oc + 1)
            sync.dma_start(out=kc_in[oc * 128:(oc + 1) * 128, :],
                           in_=kstg[oc % 2][:, :]).then_inc(SD_KSP[oc % 2], 16)
        sync.wait_ge(S.s_std, 16)
        sync.dma_start(out=st_in, in_=bst[0:1, :]).then_inc(S.sd_sti, 16)
        sync.wait_ge(S.s_pcv, 64)
        sync.dma_start(out=wvb[0][:, :], in_=wvb_d[0, :, :]).then_inc(SD_W[0], 16)
        sync.dma_start(out=wvb[1][:, :], in_=wvb_d[1, :, :]).then_inc(SD_W[1], 16)
        vc4 = vc_in.rearrange("(h p) q -> h p q", h=16)
        for dvs in range(8):
            sync.wait_ge(S.sa_vev, dvs * 16 + 16)
            for sub in range(2):
                dst = vc4[2 * dvs + sub, :, :].rearrange(
                    "p (t o) -> p t o", o=129)[:, :, 0:128]
                sync.dma_start(out=dst,
                               in_=vstgb[:, :, sub, :]).then_inc(SD_VSP[dvs % 2], 16)
            if dvs < 6:
                sync.wait_ge(S.s_pmm, 16 + 16 * (dvs + 1))
                sync.dma_start(out=wvb[dvs % 2][:, :],
                               in_=wvb_d[dvs + 2, :, :]).then_inc(SD_W[dvs % 2], 16)
        sync.wait_ge(S.s_pcq, 64)
        sync.dma_start(out=wqb[0][:, :], in_=wqb_d[0, :, :]).then_inc(SD_W[0], 16)
        sync.dma_start(out=wqb[1][:, :], in_=wqb_d[1, :, :]).then_inc(SD_W[1], 16)
        for oc in range(16):
            sync.wait_ge(S.sa_ev, 16 + oc + 1)
            sync.dma_start(out=qt_sp[oc, :, :],
                           in_=qstg[oc % 2][:, :]).then_inc(SD_QSP[oc % 2], 16)
            if oc < 14:
                sync.wait_ge(S.s_pmm, 144 + oc + 1)
                sync.dma_start(out=wqb[oc % 2][:, :],
                               in_=wqb_d[oc + 2, :, :]).then_inc(SD_W[oc % 2], 16)

        def head_loads(hf, h):
            hi = hf * 16 + h
            par = hi % 2
            for r in range(4):
                sync.dma_start(out=ksb[par][:, r * 2048:(r + 1) * 2048],
                               in_=kc_out[r * 2048 + h * 128: r * 2048 + (h + 1) * 128, :]
                               ).then_inc(SD_K[hi % 2], 16)
            vflat = vsb[par].ap().rearrange("p c o -> p (c o)")
            for r in range(4):
                sync.dma_start(out=vflat[:, r * 2064:(r + 1) * 2064],
                               in_=vc_out[r * 2048 + h * 128: r * 2048 + (h + 1) * 128, :]
                               ).then_inc(SD_V[hi % 2], 16)
            sync.dma_start(out=qsb[par][:, :],
                           in_=qt_sp[h, :, hf * 1024:(hf + 1) * 1024]
                           ).then_inc(SD_Q[hi % 2], 16)

        for hf in range(2):
            if hf == 0:
                sync.wait_ge(S.s_cc, 1)
                for r in range(4):
                    sync.dma_start(out=ksb[0][:, r * 2048:(r + 1) * 2048],
                                   in_=kc_out[r * 2048:r * 2048 + 128, :]
                                   ).then_inc(SD_K[0], 16)
                sync.wait_ge(S.s_cc, 2)
                sync.dma_start(out=stld[0:1, :],
                               in_=st_out.rearrange("(o r) h -> o (r h)", o=1)).then_inc(S.sd_sti, 16)
                sync.wait_ge(S.s_cc, 3)
                vflat0 = vsb[0].ap().rearrange("p c o -> p (c o)")
                for r in range(4):
                    sync.dma_start(out=vflat0[:, r * 2064:(r + 1) * 2064],
                                   in_=vc_out[r * 2048:r * 2048 + 128, :]
                                   ).then_inc(SD_V[0], 16)
                sync.wait_ge(S.sd_qsp0, 128)
                sync.wait_ge(S.sd_qsp1, 128)
                sync.dma_start(out=qsb[0][:, :],
                               in_=qt_sp[0, :, 0:1024]).then_inc(SD_Q[0], 16)
            if hf == 0:
                sync.wait_ge(S.s_pmm, 160)
            else:
                sync.wait_ge(S.sd_out, 128)
            for tl in range(8):
                tt = hf * 8 + tl
                sync.dma_start(out=xnew[:, tl * 2048:(tl + 1) * 2048],
                               in_=x_in[tt * 128:(tt + 1) * 128, :]).then_inc(S.sd_x, 16)
            for h in range(16):
                hi = hf * 16 + h
                if hf == 0 and h == 0:
                    continue
                if hi == 16:
                    sync.wait_ge(S.s_dnm, 512)
                if hi == 1:
                    sync.wait_ge(S.s_pmm, 160)
                if hi >= 2:
                    sync.wait_ge(S.s_ep, 2 * (hi - 1))
                head_loads(hf, h)
            if debug and hf == 0:
                sync.wait_ge(S.s_cc, 3)
                sync.dma_start(out=dbg_kc, in_=kc_out).then_inc(S.sd_dbg, 16)
                sync.dma_start(out=dbg_vc, in_=vc_out).then_inc(S.sd_dbg, 16)
                sync.dma_start(out=dbg_st, in_=st_out).then_inc(S.sd_dbg, 16)
                sync.dma_start(out=dbg_qt.rearrange("h p q -> (h p) q"),
                               in_=qt_sp.rearrange("h p q -> (h p) q")).then_inc(S.sd_dbg, 16)
            if debug:
                if TAP_EARLY:
                    sync.wait_ge(S.sa_xn, 8 * (hf + 1))
                else:
                    sync.wait_ge(S.s_ep, 32 * (hf + 1))
                for tl in range(8):
                    tt = hf * 8 + tl
                    sync.dma_start(out=dbg_xnew[tt * 128:(tt + 1) * 128, :],
                                   in_=xnew[:, tl * 2048:(tl + 1) * 2048]).then_inc(S.sd_dbg, 16)
                if hf == 0:
                    sync.dma_start(out=dbg_ast, in_=ast[0:1, :]).then_inc(S.sd_dbg, 16)
            for tl in range(8):
                tt = hf * 8 + tl
                sync.wait_ge(S.s_dnv, hf * 512 + 15 * 32 + tl * 4 + 4)
                sync.dma_start(out=out[tt * 128:(tt + 1) * 128, :],
                               in_=xnew[:, tl * 2048:(tl + 1) * 2048]).then_inc(S.sd_out, 16)
        sync.wait_ge(S.sd_out, 256)
        if debug:
            sync.wait_ge(S.sd_dbg, 16 * 21)

    def em_gpsimd(g):
        g.memset(ident[:, :], 0.0)
        g.affine_select(out=ident[:, :], in_=ident[:, :], compare_op=ALU.not_equal,
                        fill=1.0, base=0, pattern=[[-1, 128]], channel_multiplier=1)
        g.memset(ones_c[:, :], 1.0)
        g.memset(ones_r[0:1, :], 1.0).then_inc(S.s_init, 1)
        g.dma_start(out=bv_bf[0:1, :], in_=bv_h).then_inc(S.s_init, 16)
        wv_f = wv_t.rearrange("a p q -> (a p) q")
        wvd_f = wvb_d.rearrange("a p q -> (a p) q")
        wq_f = wq_t.rearrange("a p q -> (a p) q")
        wqd_f = wqb_d.rearrange("a p q -> (a p) q")
        for oc in range(16):
            if oc >= 2:
                g.wait_ge(S.s_pmm, oc - 1)
            g.dma_start(out=wkb[oc % 2][:, :], in_=wk_t[oc, :, :]).then_inc(SD_W[oc % 2], 16)
            if 2 <= oc < 6:
                c = oc - 2
                g.dma_start(out=wvd_f[c * 256:(c + 1) * 256, :],
                            in_=wv_f[c * 256:(c + 1) * 256, :]).then_inc(S.s_pcv, 16)
            elif 6 <= oc < 10:
                c = oc - 6
                g.dma_start(out=wqd_f[c * 512:(c + 1) * 512, :],
                            in_=wq_f[c * 512:(c + 1) * 512, :]).then_inc(S.s_pcq, 16)
        g.wait_ge(S.sd_ksp0, 128)
        g.wait_ge(S.sd_ksp1, 128)
        g.collective_compute("AllGather", ALU.bypass,
                             replica_groups=[[0, 1, 2, 3], [4, 5, 6, 7]],
                             ins=[kc_in], outs=[kc_out]).then_inc(S.s_cc, 1)
        g.wait_ge(S.sd_sti, 48)
        g.collective_compute("AllGather", ALU.bypass,
                             replica_groups=[[0, 1, 2, 3], [4, 5, 6, 7]],
                             ins=[st_in], outs=[st_out]).then_inc(S.s_cc, 1)
        g.wait_ge(S.sd_vsp0, 16 * 8)
        g.wait_ge(S.sd_vsp1, 16 * 8)
        g.collective_compute("AllGather", ALU.bypass,
                             replica_groups=[[0, 1, 2, 3], [4, 5, 6, 7]],
                             ins=[vc_in], outs=[vc_out]).then_inc(S.s_cc, 1)
        for hf in range(2):
            for jc in range(JC):
                jcg = hf * JC + jc
                if jc == 0:
                    g.wait_ge(S.s_ep, 32 * (hf + 1))
                    if debug:
                        g.wait_ge(S.sd_dbg, 16 * (13 if hf == 0 else 21))
                if jcg >= 2:
                    g.wait_ge(S.s_gu, 2 * (jcg - 2) + 1)
                g.dma_start(out=wgb[jc % 2][:, :], in_=wg_t[jc, :, :]).then_inc(SD_WG[jc % 2], 16)
                if jcg >= 2:
                    g.wait_ge(S.s_gu, 2 * (jcg - 2) + 2)
                g.dma_start(out=wub[jc % 2][:, :], in_=wu_t[jc, :, :]).then_inc(SD_WU[jc % 2], 16)
                g.wait_ge(S.s_dnm, hf * 512 + 32 * (jc // 4))
                g.dma_start(out=wdg[jc % 4][:, :], in_=wd_t[jc, :, :]).then_inc(S.sd_wd, 16)

    def em_vector(v):
        v.memset(epsc[:, :], EPS)
        hT3 = hT.ap().rearrange("p (c t) -> p c t", c=16)

        def h_evac(k):
            v.wait_ge(S.s_tr, k + 1)
            src = pab[:, (k % 2) * 1024:(k % 2) * 1024 + 1024]
            src = src.rearrange("p (c t) -> p c t", c=8)
            dst = hT3[:, (k % 2) * 8:(k % 2) * 8 + 8, (k // 2) * 128:(k // 2) * 128 + 128]
            v.tensor_copy(out=dst, in_=src).then_inc(S.sv_t, 1)

        def ln_pre(t, wkb_, xsl, dumpt):
            w = t * 6
            yield lambda: v.scalar_tensor_tensor(out=dumpt, in0=xsl, scalar=1.0,
                                                 in1=xsl, op0=ALU.mult, op1=ALU.mult,
                                                 accum_out=wkb_[:, w + 0:w + 1])
            yield lambda: v.tensor_single_scalar(wkb_[:, w + 2:w + 3], wkb_[:, w + 1:w + 2],
                                                 1.0 / 2048.0, ALU.mult)
            yield lambda: v.tensor_scalar(out=wkb_[:, w + 3:w + 4], in0=wkb_[:, w + 2:w + 3],
                                          scalar1=wkb_[:, w + 2:w + 3], scalar2=-1.0,
                                          op0=ALU.mult, op1=ALU.mult)
            yield lambda: v.scalar_tensor_tensor(out=wkb_[:, w + 4:w + 5],
                                                 in0=wkb_[:, w + 0:w + 1],
                                                 scalar=1.0 / 2048.0,
                                                 in1=wkb_[:, w + 3:w + 4],
                                                 op0=ALU.mult, op1=ALU.add)

        def spacer():
            return v.tensor_copy(vspc[:, 0:1], epsc[:, 0:1])

        for tp in range(0, NTT, 2):
            if tp >= 2:
                for k in (2 * (tp - 2), 2 * (tp - 2) + 1, 2 * (tp - 1), 2 * (tp - 1) + 1):
                    h_evac(k)
                v.wait_ge(S.s_tr, 2 * tp)
            v.wait_ge(S.sd_x, 16 * (tp + 2))
            v.wait_ge(S.sa_sqa, tp + 2)
            ca = ln_pre(tp, work1, xt[0][:, :], hbf[0][:, :])
            cb = ln_pre(tp + 1, work1, xt[1][:, :], hbf[1][:, :])
            for a, b in zip(ca, cb):
                a()
                b()
            spacer().then_inc(S.s_ld1, 2)
            v.wait_ge(S.sa_r1, tp + 2)
            for t in (tp, tp + 1):
                w = t * 6
                v.tensor_scalar(out=nmr1[:, t:t + 1], in0=work1[:, w + 2:w + 3],
                                scalar1=rstd1[:, t:t + 1], scalar2=-1.0,
                                op0=ALU.mult, op1=ALU.mult)
            for t in (tp, tp + 1):
                v.tensor_scalar_add(rstd1p[:, t:t + 1], rstd1[:, t:t + 1], 1.0)
            spacer().then_inc(S.s_ld2, 2)
        for k in range(2 * (NTT - 2), 2 * NTT):
            h_evac(k)
        for ph, dstc in ((0, bst), (1, ast)):
            for oc in range(16):
                v.wait_ge(S.s_stp, ph * 16 + oc + 1)
                par = oc % 2
                v.reduce_max(dstc[0:1, oc:oc + 1],
                             pa[0:1, par * 2048:par * 2048 + 2048],
                             axis=AX.X).then_inc(S.s_std, 1)

        for hf in range(2):
            if hf == 0:
                v.wait_ge(S.sd_sti, 64)
                v.tensor_tensor(bmax[0:1, :], stld[0:1, 0:16], stld[0:1, 16:32], ALU.max)
                v.tensor_tensor(ncw[0:1, 48:64], stld[0:1, 32:48], stld[0:1, 48:64], ALU.max)
                spacer()
                v.tensor_tensor(bmax[0:1, :], bmax[0:1, :], ncw[0:1, 48:64], ALU.max)
                v.wait_ge(S.s_std, 32)
                spacer()
                v.tensor_tensor(ncw[0:1, 0:16], ast[0:1, :], bmax[0:1, :], ALU.mult)
                spacer().then_inc(S.s_nc, 1)
                v.wait_ge(S.s_nc, 2)
                v.tensor_single_scalar(ncw[0:1, 32:48], ncw[0:1, 16:32], -SCALE, ALU.mult)
                spacer()
                v.tensor_copy(out=ncbf[0:1, :], in_=ncw[0:1, 32:48])
                spacer().then_inc(S.s_nc, 1)
                v.wait_ge(S.s_nc, 4)
                v.tensor_copy(out=negc[:, :], in_=pa[:, AVO:AVO + 16])
                spacer().then_inc(S.s_nc, 1)
            for h in range(16):
                hi = hf * 16 + h
                if h == 0:
                    v.wait_ge(SD_V[hi % 2], 64 * (hi // 2 + 1))
                    v.memset(vsb[hi % 2][:, :, 128:129], 1.0).then_inc(S.s_vrdy, 1)
                for nb in range(2):
                    nbi = hi * 2 + nb
                    v.wait_ge(S.s_av, NG * (nbi + 1))
                    if h == 0 and nb == 0:
                        v.wait_ge(S.sa_xn, 8 * (hf + 1))
                    for qs in range(4):
                        av = pa[:, AV_QS[qs]:AV_QS[qs] + 129]
                        v.tensor_scalar_max(ew[:, 2 * qs:2 * qs + 1],
                                            av[:, 128:129], RSFLOOR)
                    for qs in range(4):
                        v.reciprocal(ew[:, 2 * qs + 1:2 * qs + 2],
                                     ew[:, 2 * qs:2 * qs + 1])
                    for qs in range(4):
                        tl = nb * 4 + qs
                        av = pa[:, AV_QS[qs]:AV_QS[qs] + 129]
                        xsl = xnew[:, tl * 2048 + h * 128: tl * 2048 + (h + 1) * 128]
                        if BASE_ONLY:
                            ins = v.tensor_scalar_mul(dbgdump[:, :], av[:, 0:128],
                                                      ew[:, 2 * qs + 1:2 * qs + 2])
                        elif ATT_ONLY:
                            ins = v.tensor_scalar_mul(xsl, av[:, 0:128],
                                                      ew[:, 2 * qs + 1:2 * qs + 2])
                        else:
                            ins = v.scalar_tensor_tensor(out=xsl, in0=av[:, 0:128],
                                                         scalar=ew[:, 2 * qs + 1:2 * qs + 2],
                                                         in1=xsl,
                                                         op0=ALU.mult, op1=ALU.add)
                        if qs == 3:
                            ins.then_inc(S.s_ep, 1)
                    if nb == 0 and h < 15:
                        v.wait_ge(SD_V[(hi + 1) % 2], 64 * ((hi + 1) // 2 + 1))
                        v.memset(vsb[(hi + 1) % 2][:, :, 128:129], 1.0).then_inc(S.s_vrdy, 1)
            v.wait_ge(S.s_ep, 32 * (hf + 1))
            for tp in range(0, 8, 2):
                v.wait_ge(S.sa_sqa, 16 + hf * 8 + tp + 2)
                ca = ln_pre(tp, work2, xnew[:, tp * 2048:(tp + 1) * 2048], ln2scr[:, :])
                cb = ln_pre(tp + 1, work2, xnew[:, (tp + 1) * 2048:(tp + 2) * 2048],
                            ln2scr[:, :])
                for a, b in zip(ca, cb):
                    a()
                    b()
                spacer().then_inc(S.s_ld1, 2)
                v.wait_ge(S.sa_r1, 16 + hf * 8 + tp + 2)
                for t in (tp, tp + 1):
                    w = t * 6
                    v.tensor_scalar(out=nmr2[:, t:t + 1], in0=work2[:, w + 2:w + 3],
                                    scalar1=rstd2[:, t:t + 1], scalar2=-1.0,
                                    op0=ALU.mult, op1=ALU.mult)
                spacer().then_inc(S.s_ld2, 2)
            h2T3 = h2T.ap().rearrange("p (c t) -> p c t", c=16)
            for tl in range(8):
                for g2 in range(2):
                    k = 32 + hf * 16 + 2 * tl + g2
                    v.wait_ge(S.s_tr, k + 1)
                    src = pab[:, 6 * 1024 + (k % 2) * 1024: 6 * 1024 + (k % 2) * 1024 + 1024]
                    src = src.rearrange("p (c t) -> p c t", c=8)
                    dst = h2T3[:, g2 * 8:(g2 + 1) * 8, tl * 128:(tl + 1) * 128]
                    v.tensor_copy(out=dst, in_=src).then_inc(S.sv_t, 1)
            for jc in range(JC):
                jcg = hf * JC + jc
                v.wait_ge(S.sa_sil, jcg + 1)
                v.wait_ge(S.s_gu, 2 * jcg + 2)
                if jc >= 4:
                    v.wait_ge(S.s_dnm, hf * 512 + 32 * (jc // 4))
                v.tensor_tensor(gg[jc % 4][:, :], stmp[:, :], pa[:, 1024:2048],
                                ALU.mult).then_inc(S.s_gmu, 1)
                if jc % 4 == 3:
                    grp = jc // 4
                    for tt in range(8):
                        for ds in range(4):
                            kd = kd_idx(hf, grp, tt, ds)
                            v.wait_ge(S.s_dnm, kd + 1)
                            dn = pa[:, 2048 + (kd % 2) * 512: 2048 + (kd % 2) * 512 + 512]
                            xsl = xnew[:, tt * 2048 + ds * 512: tt * 2048 + (ds + 1) * 512]
                            v.tensor_tensor(xsl, dn, xsl, ALU.add).then_inc(S.s_dnv, 1)

    def em_tensor(t):
        for tt in range(NTT):
            for g2 in range(2):
                k = 2 * tt + g2
                if k == 0:
                    t.wait_ge(S.s_init, 17)
                t.wait_ge(S.sa_h, tt + 1)
                if k >= 2:
                    t.wait_ge(S.sv_t, k - 1)
                for j in range(8):
                    c = g2 * 8 + j
                    ins = t.transpose(
                        pab[:, (k % 2) * 1024 + j * 128:(k % 2) * 1024 + (j + 1) * 128],
                        hbf[tt % 2][:, c * 128:(c + 1) * 128], ident[:, :])
                    if j == 7:
                        ins.then_inc(S.s_tr, 1)
        def proj_phase(ph, wb, nsub, fw_base, pmm_base):
            for oc in range(16):
                par = oc % 2
                t.wait_ge(SD_W[oc % 2], 16 * (fw_base + oc // 2 + 1))
                if oc == 0 and ph == 0:
                    t.wait_ge(S.sv_t, 32)
                if oc == 0 and ph == 1:
                    t.wait_ge(S.sa_vev, 128)
                if oc >= 2:
                    t.wait_ge(S.s_std, ph * 16 + oc - 1)
                for fs in range(4):
                    for ic in range(16):
                        ins = t.matmul(pa[:, par * 2048 + fs * 512: par * 2048 + (fs + 1) * 512],
                                       wb[par][:, ic * 128:(ic + 1) * 128],
                                       hT[:, ic * 2048 + fs * 512: ic * 2048 + (fs + 1) * 512],
                                       start=(ic == 0), stop=(ic == 15))
                        if fs == 3 and ic == 15:
                            ins.then_inc(S.s_pmm, 1)
                if oc >= 1:
                    so = oc - 1
                    t.wait_ge(S.sa_sq, ph * 16 + so + 1)
                    sp = so % 2
                    for fs in range(4):
                        ins = t.matmul(pa[0:1, sp * 2048 + fs * 512: sp * 2048 + fs * 512 + 512],
                                       ones_c[:, 0:1], sqscr[:, fs * 512:(fs + 1) * 512],
                                       start=True, stop=True)
                        if fs == 3:
                            ins.then_inc(S.s_stp, 1)
            so = 15
            t.wait_ge(S.sa_sq, ph * 16 + so + 1)
            sp = so % 2
            for fs in range(4):
                ins = t.matmul(pa[0:1, sp * 2048 + fs * 512: sp * 2048 + fs * 512 + 512],
                               ones_c[:, 0:1], sqscr[:, fs * 512:(fs + 1) * 512],
                               start=True, stop=True)
                if fs == 3:
                    ins.then_inc(S.s_stp, 1)

        proj_phase(0, wkb, None, 0, 0)
        for dvs in range(8):
            t.wait_ge(SD_W[dvs % 2], 16 * (8 + dvs // 2 + 1))
            if dvs == 0:
                t.wait_ge(S.s_std, 16)
            for tt in range(16):
                kv = dvs * 16 + tt
                if kv >= 2:
                    t.wait_ge(S.sa_vev, kv - 1)
                pb = (kv % 2) * 512
                for ic in range(16):
                    t.matmul(pa[:, pb:pb + 256],
                             hT[:, ic * 2048 + tt * 128: ic * 2048 + (tt + 1) * 128],
                             wvb[dvs % 2][:, ic * 256:(ic + 1) * 256],
                             start=(ic == 0), stop=False)
                t.matmul(pa[:, pb:pb + 256], ones_r[0:1, :],
                         bv_bf[0:1, dvs * 256:(dvs + 1) * 256],
                         start=False, stop=True).then_inc(S.s_pmm, 1)
        proj_phase(1, wqb, None, 12, 144)

        for hf in range(2):
            for h in range(16):
                hi = hf * 16 + h
                par = hi % 2
                if hi == 16:
                    t.wait_ge(S.sa_sil, 64)
                    t.wait_ge(S.s_gmu, 64)
                    t.wait_ge(S.s_dnv, 512)
                if hi == 0:
                    t.wait_ge(S.s_nc, 3)
                    t.matmul(pa[:, AVO:AVO + 16], ones_r[0:1, :], ncbf[0:1, :],
                             start=True, stop=True).then_inc(S.s_nc, 1)
                t.wait_ge(SD_K[hi % 2], 64 * (hi // 2 + 1))
                t.wait_ge(SD_Q[hi % 2], 16 * (hi // 2 + 1))
                for nb in range(2):
                    nbi = hi * 2 + nb
                    qrhs = qsb[par][:, nb * 512:(nb + 1) * 512]

                    def sc(g):
                        base = (g % 2) * 1536
                        for j, ch in enumerate(GROUPS[g]):
                            ins = t.matmul(pa[:, base + j * 512: base + (j + 1) * 512],
                                           ksb[par][:, ch * 128:(ch + 1) * 128],
                                           qrhs, start=True, stop=True)
                            if j == len(GROUPS[g]) - 1:
                                ins.then_inc(S.s_sc, 1)

                    def av(g):
                        t.wait_ge(S.s_exp, gg_idx(hf, h, nb, g) + 1)
                        if g == 0:
                            t.wait_ge(S.s_ep, nbi)
                            t.wait_ge(S.s_vrdy, hi + 1)
                            if nbi == 0:
                                t.wait_ge(S.s_nc, 5)
                            if nbi == 32:
                                t.wait_ge(S.sv_t, 48)
                        base = (gg_idx(hf, h, nb, g) % 2)
                        for j, ch in enumerate(GROUPS[g]):
                            for qs in range(4):
                                ins = t.matmul(pa[:, AV_QS[qs]:AV_QS[qs] + 129],
                                               psl[base][:, j * 512 + qs * 128: j * 512 + (qs + 1) * 128],
                                               vsb[par][:, ch, :],
                                               start=(ch == 0 and qs in (0, 2)),
                                               stop=(ch == 63),
                                               skip_group_check=True)
                                if j == len(GROUPS[g]) - 1 and qs == 3:
                                    ins.then_inc(S.s_av, 1)

                    sc(0)
                    for k in range(NG - 1):
                        sc(k + 1)
                        av(k)
                    av(NG - 1)
            for tl in range(8):
                for g2 in range(2):
                    k = 32 + hf * 16 + 2 * tl + g2
                    if tl == 0 and g2 == 0:
                        t.wait_ge(S.s_ep, 32 * (hf + 1))
                    t.wait_ge(S.sa_h, 17 + hf * 8 + tl)
                    if not (tl == 0 and g2 == 0):
                        t.wait_ge(S.sv_t, k - 1)
                    for j in range(8):
                        c = g2 * 8 + j
                        ins = t.transpose(
                            pab[:, 6 * 1024 + (k % 2) * 1024 + j * 128:
                                6 * 1024 + (k % 2) * 1024 + (j + 1) * 128],
                            h2bf[tl % 2][:, c * 128:(c + 1) * 128], ident[:, :])
                        if j == 7:
                            ins.then_inc(S.s_tr, 1)
            def emit_down(grp):
                t.wait_ge(S.sd_wd, 16 * (hf * JC + 4 * grp + 4))
                t.wait_ge(S.s_gmu, hf * JC + 4 * grp + 4)
                for tt in range(8):
                    for ds in range(4):
                        kd = kd_idx(hf, grp, tt, ds)
                        if kd >= 2:
                            t.wait_ge(S.s_dnv, kd - 1)
                        for jl in range(4):
                            ins = t.matmul(
                                pa[:, 2048 + (kd % 2) * 512: 2048 + (kd % 2) * 512 + 512],
                                gg[jl][:, tt * 128:(tt + 1) * 128],
                                wdg[jl][:, ds * 512:(ds + 1) * 512],
                                start=(jl == 0), stop=(jl == 3))
                            if jl == 3:
                                ins.then_inc(S.s_dnm, 1)

            for jc in range(JC):
                jcg = hf * JC + jc
                t.wait_ge(SD_WG[jc % 2], 16 * (hf * 32 + jc // 2 + 1))
                t.wait_ge(S.sa_sil, jcg)
                if jc == 0:
                    t.wait_ge(S.sv_t, 32 + 16 * (hf + 1))
                for ts in range(2):
                    for ic in range(16):
                        ins = t.matmul(pa[:, ts * 512:(ts + 1) * 512],
                                       wgb[jc % 2][:, ic * 128:(ic + 1) * 128],
                                       h2T[:, ic * 1024 + ts * 512: ic * 1024 + (ts + 1) * 512],
                                       start=(ic == 0), stop=(ic == 15))
                        if ts == 1 and ic == 15:
                            ins.then_inc(S.s_gu, 1)
                t.wait_ge(SD_WU[jc % 2], 16 * (hf * 32 + jc // 2 + 1))
                t.wait_ge(S.s_gmu, jcg)
                for ts in range(2):
                    for ic in range(16):
                        ins = t.matmul(pa[:, 1024 + ts * 512: 1024 + (ts + 1) * 512],
                                       wub[jc % 2][:, ic * 128:(ic + 1) * 128],
                                       h2T[:, ic * 1024 + ts * 512: ic * 1024 + (ts + 1) * 512],
                                       start=(ic == 0), stop=(ic == 15))
                        if ts == 1 and ic == 15:
                            ins.then_inc(S.s_gu, 1)
                if jc % 4 == 0 and jc >= 4:
                    emit_down(jc // 4 - 1)
            emit_down(15)

    def em_scalar(s):
        for tp in range(0, NTT, 2):
            s.wait_ge(S.sd_x, 16 * (tp + 2))
            for t in (tp, tp + 1):
                w = t * 6
                s.activation(qstg[t % 2][:, :], xt[t % 2][:, :], AF.Identity,
                             accum_out=work1[:, w + 1:w + 2]).then_inc(S.sa_sqa, 1)
            s.wait_ge(S.s_ld1, tp + 2)
            for t in (tp, tp + 1):
                w = t * 6
                s.activation(work1[:, w + 5:w + 6], work1[:, w + 4:w + 5], AF.Ln,
                             bias=epsc[:, 0:1])
                s.activation(rstd1[:, t:t + 1], work1[:, w + 5:w + 6], AF.Exp,
                             scale=-0.5).then_inc(S.sa_r1, 1)
            s.wait_ge(S.s_ld2, tp + 2)
            for t in (tp, tp + 1):
                s.activation(hbf[t % 2][:, :], xt[t % 2][:, :], AF.Identity,
                             bias=nmr1[:, t:t + 1],
                             scale=rstd1[:, t:t + 1]).then_inc(S.sa_h, 1)
        def ev_phase(ph, stg, b_sb, spill_sems):
            for oc in range(16):
                par = oc % 2
                s.wait_ge(S.s_pmm, (0 if ph == 0 else 144) + oc + 1)
                if oc >= 2:
                    s.wait_ge(spill_sems[oc % 2], 16 * (oc // 2))
                if oc == 0 and ph == 0:
                    s.wait_ge(S.sd_sti, 32)
                s.activation(stg[par][:, :], pa[:, par * 2048: par * 2048 + 2048],
                             AF.Identity, bias=b_sb[:, oc:oc + 1]).then_inc(S.sa_ev, 1)
                if oc >= 1:
                    s.wait_ge(S.s_stp, ph * 16 + oc)
                s.activation(sqscr[:, :], pa[:, par * 2048: par * 2048 + 2048],
                             AF.Square, bias=b_sb[:, oc:oc + 1]).then_inc(S.sa_sq, 1)
        ev_phase(0, kstg, bk_sb, SD_KSP)
        for dvs in range(8):
            for tt in range(16):
                kv = dvs * 16 + tt
                s.wait_ge(S.s_pmm, 16 + kv + 1)
                if tt == 0 and dvs >= 1:
                    s.wait_ge(SD_VSP[(dvs - 1) % 2], 32 * ((dvs - 1) // 2 + 1))
                pb = (kv % 2) * 512
                s.activation(vstgb[:, tt, :, :].rearrange("p s o -> p (s o)"),
                             pa[:, pb:pb + 256],
                             AF.Identity).then_inc(S.sa_vev, 1)
        ev_phase(1, qstg, bq_sb, SD_QSP)
        for hf in range(2):
            s.wait_ge(S.sd_x, 256 + 128 * (hf + 1))
            for tl in range(8):
                tt = hf * 8 + tl
                xsl = xnew[:, tl * 2048:(tl + 1) * 2048]
                s.activation(xsl, xsl, AF.Identity,
                             scale=rstd1p[:, tt:tt + 1],
                             bias=nmr1[:, tt:tt + 1]).then_inc(S.sa_xn, 1)
            if hf == 0:
                s.wait_ge(S.s_nc, 1)
                s.activation(ncw[0:1, 16:32], ncw[0:1, 0:16], AF.Ln)
                s.activation(ncw[0:1, 16:32], ncw[0:1, 16:32], AF.Exp,
                             scale=0.5).then_inc(S.s_nc, 1)
            for h in range(16):
                for nb in range(2):
                    for g in range(NG):
                        gid = gg_idx(hf, h, nb, g)
                        s.wait_ge(S.s_sc, gid + 1)
                        if gid >= 2:
                            s.wait_ge(S.s_av, gid - 1)
                        if gid == 0:
                            s.wait_ge(S.s_nc, 5)
                        cols = len(GROUPS[g]) * 512
                        base = (g % 2) * 1536
                        s.activation(psl[gid % 2][:, 0:cols], pa[:, base:base + cols],
                                     AF.Exp, bias=negc[:, h:h + 1],
                                     scale=SCALE).then_inc(S.s_exp, 1)
            s.wait_ge(S.s_ep, 32 * (hf + 1))
            for tp in range(0, 8, 2):
                for t in (tp, tp + 1):
                    w = t * 6
                    s.activation(h2bf[t % 2][:, :], xnew[:, t * 2048:(t + 1) * 2048],
                                 AF.Identity,
                                 accum_out=work2[:, w + 1:w + 2]).then_inc(S.sa_sqa, 1)
            for tp in range(0, 8, 2):
                s.wait_ge(S.s_ld1, 16 + hf * 8 + tp + 2)
                for t in (tp, tp + 1):
                    w = t * 6
                    s.activation(work2[:, w + 5:w + 6], work2[:, w + 4:w + 5], AF.Ln,
                                 bias=epsc[:, 0:1])
                    s.activation(rstd2[:, t:t + 1], work2[:, w + 5:w + 6], AF.Exp,
                                 scale=-0.5).then_inc(S.sa_r1, 1)
            for tl in range(8):
                s.wait_ge(S.s_ld2, 16 + hf * 8 + (tl - tl % 2) + 2)
                if tl >= 2:
                    s.wait_ge(S.s_tr, 32 + hf * 16 + 2 * (tl - 1))
                xsl = xnew[:, tl * 2048:(tl + 1) * 2048]
                s.activation(h2bf[tl % 2][:, :], xsl, AF.Identity,
                             scale=rstd2[:, tl:tl + 1],
                             bias=nmr2[:, tl:tl + 1]).then_inc(S.sa_h, 1)
            for jc in range(JC):
                jcg = hf * JC + jc
                s.wait_ge(S.s_gu, 2 * jcg + 1)
                s.wait_ge(S.s_gmu, jcg)
                s.activation(stmp[:, :], pa[:, 0:1024], AF.Silu).then_inc(S.sa_sil, 1)

    with nc.Block() as block:
        block.sync(em_sync)
        block.gpsimd(em_gpsimd)
        block.vector(em_vector)
        block.tensor(em_tensor)
        block.scalar(em_scalar)

    return nc


def _stage_inputs(inputs):
    x = np.asarray(inputs["L0_attn_x"], dtype=np.float32)
    Wq = np.asarray(inputs["L0_attn_Wq"], dtype=np.float32)
    Wk = np.asarray(inputs["L0_attn_Wk"], dtype=np.float32)
    Wv = np.asarray(inputs["L0_attn_Wv"], dtype=np.float32)
    bq = np.asarray(inputs["L0_attn_bq"], dtype=np.float32)
    bk = np.asarray(inputs["L0_attn_bk"], dtype=np.float32)
    bv = np.asarray(inputs["L0_attn_bv"], dtype=np.float32)
    wg = np.asarray(inputs["L0_ffn_w_gate"], dtype=np.float32)
    wu = np.asarray(inputs["L0_ffn_w_up"], dtype=np.float32)
    wd = np.asarray(inputs["L0_ffn_w_down"], dtype=np.float32)

    wq_t = np.ascontiguousarray(Wq.reshape(16, 128, 16, 128).transpose(2, 1, 0, 3)
                                ).reshape(16, 128, 2048)
    wk_t = np.ascontiguousarray(Wk.reshape(16, 128, 16, 128).transpose(2, 1, 0, 3)
                                ).reshape(16, 128, 2048)
    wv_t = np.ascontiguousarray(Wv.reshape(16, 128, 8, 256).transpose(2, 1, 0, 3)
                                ).reshape(8, 128, 4096)
    wg_t = np.ascontiguousarray(wg.reshape(64, 128, 16, 128).transpose(0, 3, 2, 1)
                                ).reshape(64, 128, 2048)
    wu_t = np.ascontiguousarray(wu.reshape(64, 128, 16, 128).transpose(0, 3, 2, 1)
                                ).reshape(64, 128, 2048)
    wd_t = np.ascontiguousarray(wd.T).reshape(64, 128, 2048)
    bq_h = np.ascontiguousarray(bq.reshape(16, 128).T)
    bk_h = np.ascontiguousarray(bk.reshape(16, 128).T)
    bv_h = np.ascontiguousarray(bv.reshape(1, 2048))

    shared = dict(wq_t=wq_t, wk_t=wk_t, wv_t=wv_t, bq_h=bq_h, bk_h=bk_h, bv_h=bv_h,
                  wg_t=wg_t, wu_t=wu_t, wd_t=wd_t)
    in_maps = []
    for c in range(8):
        b, q = c // 4, c % 4
        xs = np.ascontiguousarray(x[b, q * T:(q + 1) * T, :])
        m = {"x": xs}
        m.update(shared)
        in_maps.append(m)
    return in_maps


def kernel(**inputs):
    if "fused" not in _NC_CACHE:
        _NC_CACHE["fused"] = build_program(debug=False)
    nc = _NC_CACHE["fused"]
    in_maps = _stage_inputs(inputs)
    res = run_bass_kernel_spmd(nc, in_maps, core_ids=list(range(8)))
    out = np.empty((2, 8192, 2048), dtype=np.float32)
    for c in range(8):
        b, q = c // 4, c % 4
        out[b, q * T:(q + 1) * T, :] = res.results[c]["out"]
    return out


# version 10
# speedup vs baseline: 1.0616x; 1.0616x over previous
import math
import numpy as np
import ml_dtypes
import concourse.bass as bass
import concourse.mybir as mybir
from concourse.bass_utils import run_bass_kernel_spmd

F32 = mybir.dt.float32
BF16 = mybir.dt.bfloat16
AF = mybir.ActivationFunctionType
ALU = mybir.AluOpType
AX = mybir.AxisListType

D = 2048
T = 2048
NTT = 16
H = 16
DH = 128
FF = 8192
SFULL = 8192
MC = 64
NBH = 2
GRP = 3
NG = 22
JC = 64
DNG = 16
SCALE = 1.0 / math.sqrt(128.0)
EPS = 1e-5
RSFLOOR = 1.2e-38

GROUPS = [list(range(g * GRP, min((g + 1) * GRP, MC))) for g in range(NG)]
assert sum(len(g) for g in GROUPS) == MC and len(GROUPS[-1]) == 1

R1 = 81888
R2 = 147424
O_XT = (0, 8192)
O_WQB = (16384, 20480)
O_WKB = (24576, 28672)
O_WVB = (32768, 40960)
O_KSTG = (49152, 53248)
O_QSTG = (57344, 61440)
O_VSTG = (65536, 66048, 80128, 80640)
O_SQSCR = 66560
O_KSB = (0, 16384)
O_VSB = (32768, 49280)
O_QSB = (76032, 78080)
O_PSL = (69888, 72960)
O_H2T = 0
O_GG = (32768, 34816, 36864, 38912)
O_LN2SCR = 32768
O_WGB = (40960, 45056)
O_WUB = (49152, 53248)
O_WDG = (57344, 61440, 65536, 69632)
O_STMP = 73728
O_H2BF = (73728, 77824)

_NC_CACHE = {}
ATT_ONLY = False
BASE_ONLY = False
TAP_EARLY = False


def _sb_at(nc, name, shape, dtype, off):
    return nc.alloc_sbuf_tensor_at(name, shape, dtype, offset=off)


def build_program(debug=False):
    nc = bass.Bass("TRN2", target_bir_lowering=False)

    x_in = nc.dram_tensor("x", [T, D], F32, kind="ExternalInput").ap()
    wq_t = nc.dram_tensor("wq_t", [16, 128, 2048], F32, kind="ExternalInput").ap()
    wk_t = nc.dram_tensor("wk_t", [16, 128, 2048], F32, kind="ExternalInput").ap()
    wv_t = nc.dram_tensor("wv_t", [8, 128, 4096], F32, kind="ExternalInput").ap()
    bq_h = nc.dram_tensor("bq_h", [128, 16], F32, kind="ExternalInput").ap()
    bk_h = nc.dram_tensor("bk_h", [128, 16], F32, kind="ExternalInput").ap()
    bv_h = nc.dram_tensor("bv_h", [1, 2048], F32, kind="ExternalInput").ap()
    wg_t = nc.dram_tensor("wg_t", [64, 128, 2048], F32, kind="ExternalInput").ap()
    wu_t = nc.dram_tensor("wu_t", [64, 128, 2048], F32, kind="ExternalInput").ap()
    wd_t = nc.dram_tensor("wd_t", [64, 128, 2048], F32, kind="ExternalInput").ap()
    out = nc.dram_tensor("out", [T, D], F32, kind="ExternalOutput").ap()

    wvb_d = nc.dram_tensor("wvb_d", [8, 128, 4096], BF16, kind="Internal").ap()
    wqb_d = nc.dram_tensor("wqb_d", [16, 128, 2048], BF16, kind="Internal").ap()
    qt_sp = nc.dram_tensor("qt_sp", [16, 128, 2048], BF16, kind="Internal").ap()
    kc_in = nc.dram_tensor("kc_in", [2048, 2048], BF16, kind="Internal").ap()
    kc_out = nc.dram_tensor("kc_out", [8192, 2048], BF16, kind="Internal").ap()
    vc_in = nc.dram_tensor("vc_in", [2048, 2064], BF16, kind="Internal").ap()
    vc_out = nc.dram_tensor("vc_out", [8192, 2064], BF16, kind="Internal").ap()
    st_in = nc.dram_tensor("st_in", [1, 16], F32, kind="Internal").ap()
    st_out = nc.dram_tensor("st_out", [4, 16], F32, kind="Internal").ap()
    if debug:
        dbg_xnew = nc.dram_tensor("dbg_xnew", [T, D], F32, kind="ExternalOutput").ap()
        dbg_ast = nc.dram_tensor("dbg_ast", [1, 16], F32, kind="ExternalOutput").ap()
        dbg_qt = nc.dram_tensor("dbg_qt", [16, 128, 2048], BF16, kind="ExternalOutput").ap()
        dbg_kc = nc.dram_tensor("dbg_kc", [8192, 2048], BF16, kind="ExternalOutput").ap()
        dbg_vc = nc.dram_tensor("dbg_vc", [8192, 2064], BF16, kind="ExternalOutput").ap()
        dbg_st = nc.dram_tensor("dbg_st", [4, 16], F32, kind="ExternalOutput").ap()

    ident = nc.alloc_sbuf_tensor("ident", [128, 128], BF16)
    ones_c = nc.alloc_sbuf_tensor("ones_c", [128, 1], BF16)
    ones_r = nc.alloc_sbuf_tensor("ones_r", [1, 128], BF16)
    bq_sb = nc.alloc_sbuf_tensor("bq_sb", [128, 16], F32)
    bk_sb = nc.alloc_sbuf_tensor("bk_sb", [128, 16], F32)
    bv_bf = nc.alloc_sbuf_tensor("bv_bf", [1, 2048], BF16)
    rstd1 = nc.alloc_sbuf_tensor("rstd1", [128, 16], F32)
    nmr1 = nc.alloc_sbuf_tensor("nmr1", [128, 16], F32)
    rstd1p = nc.alloc_sbuf_tensor("rstd1p", [128, 16], F32)
    rstd2 = nc.alloc_sbuf_tensor("rstd2", [128, 8], F32)
    nmr2 = nc.alloc_sbuf_tensor("nmr2", [128, 8], F32)
    work1 = nc.alloc_sbuf_tensor("work1", [128, 16 * 6], F32)
    work2 = nc.alloc_sbuf_tensor("work2", [128, 8 * 6], F32)
    ast = nc.alloc_sbuf_tensor("ast", [1, 16], F32)
    bst = nc.alloc_sbuf_tensor("bst", [1, 16], F32)
    stld = nc.alloc_sbuf_tensor("stld", [1, 64], F32)
    bmax = nc.alloc_sbuf_tensor("bmax", [1, 16], F32)
    ncw = nc.alloc_sbuf_tensor("ncw", [1, 64], F32)
    ncbf = nc.alloc_sbuf_tensor("ncbf", [1, 16], BF16)
    negc = nc.alloc_sbuf_tensor("negc", [128, 16], F32)
    ew = nc.alloc_sbuf_tensor("ew", [128, 8], F32)
    vspc = nc.alloc_sbuf_tensor("vspc", [128, 2], F32)
    dbgdump = nc.alloc_sbuf_tensor("dbgdump", [128, 128], F32) if (ATT_ONLY or BASE_ONLY or True) else None
    epsc = nc.alloc_sbuf_tensor("epsc", [128, 1], F32)

    hT = _sb_at(nc, "hT", [128, 16 * 2048], BF16, R1)
    xnew = _sb_at(nc, "xnew", [128, 8 * 2048], F32, R1)
    xt = [_sb_at(nc, f"xt{i}", [128, 2048], F32, R2 + O_XT[i]) for i in range(2)]
    wqb = [_sb_at(nc, f"wqb{i}", [128, 2048], BF16, R2 + O_WQB[i]) for i in range(2)]
    wkb = [_sb_at(nc, f"wkb{i}", [128, 2048], BF16, R2 + O_WKB[i]) for i in range(2)]
    wvb = [_sb_at(nc, f"wvb{i}", [128, 4096], BF16, R2 + O_WVB[i]) for i in range(2)]
    kstg = [_sb_at(nc, f"kstg{i}", [128, 2048], BF16, R2 + O_KSTG[i]) for i in range(2)]
    qstg = [_sb_at(nc, f"qstg{i}", [128, 2048], BF16, R2 + O_QSTG[i]) for i in range(2)]
    hbf = kstg
    vstg = [_sb_at(nc, f"vstg{i}", [128, 256], BF16, R2 + O_VSTG[i]) for i in range(4)]
    sqscr = _sb_at(nc, "sqscr", [128, 2048], BF16, R2 + O_SQSCR)
    ksb = [_sb_at(nc, f"ksb{i}", [128, 8192], BF16, R2 + O_KSB[i]) for i in range(2)]
    vsb = [_sb_at(nc, f"vsb{i}", [128, 64, 129], BF16, R2 + O_VSB[i]) for i in range(2)]
    qsb = [_sb_at(nc, f"qsb{i}", [128, 1024], BF16, R2 + O_QSB[i]) for i in range(2)]
    psl = [_sb_at(nc, f"psl{i}", [128, 1536], BF16, R2 + O_PSL[i]) for i in range(2)]
    h2T = _sb_at(nc, "h2T", [128, 16 * 1024], BF16, R2 + O_H2T)
    gg = [_sb_at(nc, f"gg{i}", [128, 1024], BF16, R2 + O_GG[i]) for i in range(4)]
    ln2scr = _sb_at(nc, "ln2scr", [128, 2048], BF16, R2 + O_LN2SCR)
    wgb = [_sb_at(nc, f"wgb{i}", [128, 2048], BF16, R2 + O_WGB[i]) for i in range(2)]
    wub = [_sb_at(nc, f"wub{i}", [128, 2048], BF16, R2 + O_WUB[i]) for i in range(2)]
    wdg = [_sb_at(nc, f"wdg{i}", [128, 2048], BF16, R2 + O_WDG[i]) for i in range(4)]
    stmp = _sb_at(nc, "stmp", [128, 1024], F32, R2 + O_STMP)
    h2bf = [_sb_at(nc, f"h2bf{i}", [128, 2048], BF16, R2 + O_H2BF[i]) for i in range(2)]

    pa = nc.alloc_psum_tensor("pa", [128, 4096], F32)
    pab = pa.bitcast(BF16)
    AVO = 3072
    AV_QS = [AVO + 0, AVO + 129, AVO + 512, AVO + 641]

    semnames = ["sd_x", "sa_h", "s_tr", "sv_t", "s_init", "s_ld1", "sa_r1", "s_ld2",
                "sd_w0", "sd_w1", "s_pmm", "sa_ev", "sa_sq", "s_stp", "s_std",
                "sd_ksp0", "sd_ksp1", "sd_qsp0", "sd_qsp1", "sd_sti",
                "sa_vev", "sd_vsp0", "sd_vsp1", "sd_vsp2", "sd_vsp3",
                "s_cc", "sa_xn", "s_nc",
                "sd_k0", "sd_k1", "sd_v0", "sd_v1", "sd_q0", "sd_q1",
                "s_vrdy", "s_sc", "s_exp", "s_av", "s_ep",
                "sd_wg0", "sd_wg1", "sd_wu0", "sd_wu1", "sd_wd", "s_gu", "sa_sil",
                "s_gmu", "s_dnm", "s_dnv", "sd_out", "sd_dbg", "s_pcv", "s_pcq", "sa_sqa"]
    sems = {n: nc.alloc_semaphore(n) for n in semnames}
    S = type("S", (), sems)
    SD_W = [S.sd_w0, S.sd_w1]
    SD_KSP = [S.sd_ksp0, S.sd_ksp1]
    SD_QSP = [S.sd_qsp0, S.sd_qsp1]
    SD_VSP = [S.sd_vsp0, S.sd_vsp1, S.sd_vsp2, S.sd_vsp3]
    SD_K = [S.sd_k0, S.sd_k1]
    SD_V = [S.sd_v0, S.sd_v1]
    SD_Q = [S.sd_q0, S.sd_q1]
    SD_WG = [S.sd_wg0, S.sd_wg1]
    SD_WU = [S.sd_wu0, S.sd_wu1]

    def gg_idx(hf, h, nb, g):
        return (((hf * 16 + h) * 2) + nb) * NG + g

    def kd_idx(hf, grp, tt, ds):
        return hf * 512 + grp * 32 + tt * 4 + ds

    dbg_sync_extra = []

    def em_sync(sync):
        sync.dma_start(out=bq_sb[:, :], in_=bq_h).then_inc(S.sd_sti, 16)
        sync.dma_start(out=bk_sb[:, :], in_=bk_h).then_inc(S.sd_sti, 16)
        for tt in range(NTT):
            if tt >= 2:
                sync.wait_ge(S.sa_h, tt - 1)
            sync.dma_start(out=xt[tt % 2][:, :],
                           in_=x_in[tt * 128:(tt + 1) * 128, :]).then_inc(S.sd_x, 16)
        for oc in range(16):
            sync.wait_ge(S.sa_ev, oc + 1)
            sync.dma_start(out=kc_in[oc * 128:(oc + 1) * 128, :],
                           in_=kstg[oc % 2][:, :]).then_inc(SD_KSP[oc % 2], 16)
        sync.wait_ge(S.s_std, 16)
        sync.dma_start(out=st_in, in_=bst[0:1, :]).then_inc(S.sd_sti, 16)
        sync.wait_ge(S.s_pcv, 64)
        sync.dma_start(out=wvb[0][:, :], in_=wvb_d[0, :, :]).then_inc(SD_W[0], 16)
        sync.dma_start(out=wvb[1][:, :], in_=wvb_d[1, :, :]).then_inc(SD_W[1], 16)
        vc4 = vc_in.rearrange("(h p) q -> h p q", h=16)
        for dvs in range(8):
            for tt in range(16):
                kv = dvs * 16 + tt
                sync.wait_ge(S.sa_vev, kv + 1)
                dst = vc4[2 * dvs:2 * dvs + 2, :, tt * 129:tt * 129 + 128].rearrange(
                    "s p o -> p s o")
                src = vstg[kv % 4][:, :].rearrange("p (s o) -> p s o", s=2)
                sync.dma_start(out=dst, in_=src).then_inc(SD_VSP[kv % 4], 16)
            if dvs < 6:
                sync.wait_ge(S.s_pmm, 16 + 16 * (dvs + 1))
                sync.dma_start(out=wvb[dvs % 2][:, :],
                               in_=wvb_d[dvs + 2, :, :]).then_inc(SD_W[dvs % 2], 16)
        sync.wait_ge(S.s_pcq, 64)
        sync.dma_start(out=wqb[0][:, :], in_=wqb_d[0, :, :]).then_inc(SD_W[0], 16)
        sync.dma_start(out=wqb[1][:, :], in_=wqb_d[1, :, :]).then_inc(SD_W[1], 16)
        for oc in range(16):
            sync.wait_ge(S.sa_ev, 16 + oc + 1)
            sync.dma_start(out=qt_sp[oc, :, :],
                           in_=qstg[oc % 2][:, :]).then_inc(SD_QSP[oc % 2], 16)
            if oc < 14:
                sync.wait_ge(S.s_pmm, 144 + oc + 1)
                sync.dma_start(out=wqb[oc % 2][:, :],
                               in_=wqb_d[oc + 2, :, :]).then_inc(SD_W[oc % 2], 16)

        for hf in range(2):
            if hf == 0:
                sync.wait_ge(S.s_pmm, 160)
            else:
                sync.wait_ge(S.sd_out, 128)
            for tl in range(8):
                tt = hf * 8 + tl
                sync.dma_start(out=xnew[:, tl * 2048:(tl + 1) * 2048],
                               in_=x_in[tt * 128:(tt + 1) * 128, :]).then_inc(S.sd_x, 16)
            if hf == 0:
                sync.wait_ge(S.s_cc, 2)
                sync.dma_start(out=stld[0:1, :],
                               in_=st_out.rearrange("(o r) h -> o (r h)", o=1)).then_inc(S.sd_sti, 16)
            for h in range(16):
                hi = hf * 16 + h
                if hi == 0:
                    sync.wait_ge(S.s_cc, 1)
                    sync.wait_ge(S.sd_qsp0, 128)
                    sync.wait_ge(S.sd_qsp1, 128)
                if hi == 16:
                    sync.wait_ge(S.s_dnm, 512)
                if hi >= 2:
                    sync.wait_ge(S.s_ep, 2 * (hi - 1))
                par = hi % 2
                for r in range(4):
                    sync.dma_start(out=ksb[par][:, r * 2048:(r + 1) * 2048],
                                   in_=kc_out[r * 2048 + h * 128: r * 2048 + (h + 1) * 128, :]
                                   ).then_inc(SD_K[hi % 2], 16)
                if hi == 0:
                    sync.wait_ge(S.s_cc, 3)
                vflat = vsb[par].ap().rearrange("p c o -> p (c o)")
                for r in range(4):
                    sync.dma_start(out=vflat[:, r * 2064:(r + 1) * 2064],
                                   in_=vc_out[r * 2048 + h * 128: r * 2048 + (h + 1) * 128, :]
                                   ).then_inc(SD_V[hi % 2], 16)
                sync.dma_start(out=qsb[par][:, :],
                               in_=qt_sp[h, :, hf * 1024:(hf + 1) * 1024]
                               ).then_inc(SD_Q[hi % 2], 16)
            if debug and hf == 0:
                sync.wait_ge(S.s_cc, 3)
                sync.dma_start(out=dbg_kc, in_=kc_out).then_inc(S.sd_dbg, 16)
                sync.dma_start(out=dbg_vc, in_=vc_out).then_inc(S.sd_dbg, 16)
                sync.dma_start(out=dbg_st, in_=st_out).then_inc(S.sd_dbg, 16)
                sync.dma_start(out=dbg_qt.rearrange("h p q -> (h p) q"),
                               in_=qt_sp.rearrange("h p q -> (h p) q")).then_inc(S.sd_dbg, 16)
            if debug:
                if TAP_EARLY:
                    sync.wait_ge(S.sa_xn, 8 * (hf + 1))
                else:
                    sync.wait_ge(S.s_ep, 32 * (hf + 1))
                for tl in range(8):
                    tt = hf * 8 + tl
                    sync.dma_start(out=dbg_xnew[tt * 128:(tt + 1) * 128, :],
                                   in_=xnew[:, tl * 2048:(tl + 1) * 2048]).then_inc(S.sd_dbg, 16)
                if hf == 0:
                    sync.dma_start(out=dbg_ast, in_=ast[0:1, :]).then_inc(S.sd_dbg, 16)
            for tl in range(8):
                tt = hf * 8 + tl
                sync.wait_ge(S.s_dnv, hf * 512 + 15 * 32 + tl * 4 + 4)
                sync.dma_start(out=out[tt * 128:(tt + 1) * 128, :],
                               in_=xnew[:, tl * 2048:(tl + 1) * 2048]).then_inc(S.sd_out, 16)
        sync.wait_ge(S.sd_out, 256)
        if debug:
            sync.wait_ge(S.sd_dbg, 16 * 21)

    def em_gpsimd(g):
        g.memset(ident[:, :], 0.0)
        g.affine_select(out=ident[:, :], in_=ident[:, :], compare_op=ALU.not_equal,
                        fill=1.0, base=0, pattern=[[-1, 128]], channel_multiplier=1)
        g.memset(ones_c[:, :], 1.0)
        g.memset(ones_r[0:1, :], 1.0).then_inc(S.s_init, 1)
        g.dma_start(out=bv_bf[0:1, :], in_=bv_h).then_inc(S.s_init, 16)
        wv_f = wv_t.rearrange("a p q -> (a p) q")
        wvd_f = wvb_d.rearrange("a p q -> (a p) q")
        wq_f = wq_t.rearrange("a p q -> (a p) q")
        wqd_f = wqb_d.rearrange("a p q -> (a p) q")
        for oc in range(16):
            if oc >= 2:
                g.wait_ge(S.s_pmm, oc - 1)
            g.dma_start(out=wkb[oc % 2][:, :], in_=wk_t[oc, :, :]).then_inc(SD_W[oc % 2], 16)
            if 2 <= oc < 6:
                c = oc - 2
                g.dma_start(out=wvd_f[c * 256:(c + 1) * 256, :],
                            in_=wv_f[c * 256:(c + 1) * 256, :]).then_inc(S.s_pcv, 16)
            elif 6 <= oc < 10:
                c = oc - 6
                g.dma_start(out=wqd_f[c * 512:(c + 1) * 512, :],
                            in_=wq_f[c * 512:(c + 1) * 512, :]).then_inc(S.s_pcq, 16)
        g.wait_ge(S.sd_ksp0, 128)
        g.wait_ge(S.sd_ksp1, 128)
        g.collective_compute("AllGather", ALU.bypass,
                             replica_groups=[[0, 1, 2, 3], [4, 5, 6, 7]],
                             ins=[kc_in], outs=[kc_out]).then_inc(S.s_cc, 1)
        g.wait_ge(S.sd_sti, 48)
        g.collective_compute("AllGather", ALU.bypass,
                             replica_groups=[[0, 1, 2, 3], [4, 5, 6, 7]],
                             ins=[st_in], outs=[st_out]).then_inc(S.s_cc, 1)
        for vs in SD_VSP:
            g.wait_ge(vs, 16 * 32)
        g.collective_compute("AllGather", ALU.bypass,
                             replica_groups=[[0, 1, 2, 3], [4, 5, 6, 7]],
                             ins=[vc_in], outs=[vc_out]).then_inc(S.s_cc, 1)
        for hf in range(2):
            for jc in range(JC):
                jcg = hf * JC + jc
                if jc == 0:
                    g.wait_ge(S.s_ep, 32 * (hf + 1))
                    if debug:
                        g.wait_ge(S.sd_dbg, 16 * (13 if hf == 0 else 21))
                if jcg >= 2:
                    g.wait_ge(S.s_gu, 2 * (jcg - 2) + 1)
                g.dma_start(out=wgb[jc % 2][:, :], in_=wg_t[jc, :, :]).then_inc(SD_WG[jc % 2], 16)
                if jcg >= 2:
                    g.wait_ge(S.s_gu, 2 * (jcg - 2) + 2)
                g.dma_start(out=wub[jc % 2][:, :], in_=wu_t[jc, :, :]).then_inc(SD_WU[jc % 2], 16)
                g.wait_ge(S.s_dnm, hf * 512 + 32 * (jc // 4))
                g.dma_start(out=wdg[jc % 4][:, :], in_=wd_t[jc, :, :]).then_inc(S.sd_wd, 16)

    def em_vector(v):
        v.memset(epsc[:, :], EPS)
        hT3 = hT.ap().rearrange("p (c t) -> p c t", c=16)

        def h_evac(k):
            v.wait_ge(S.s_tr, k + 1)
            src = pab[:, (k % 2) * 1024:(k % 2) * 1024 + 1024]
            src = src.rearrange("p (c t) -> p c t", c=8)
            dst = hT3[:, (k % 2) * 8:(k % 2) * 8 + 8, (k // 2) * 128:(k // 2) * 128 + 128]
            v.tensor_copy(out=dst, in_=src).then_inc(S.sv_t, 1)

        def ln_pre(t, wkb_, xsl, dumpt):
            w = t * 6
            yield lambda: v.scalar_tensor_tensor(out=dumpt, in0=xsl, scalar=1.0,
                                                 in1=xsl, op0=ALU.mult, op1=ALU.mult,
                                                 accum_out=wkb_[:, w + 0:w + 1])
            yield lambda: v.tensor_single_scalar(wkb_[:, w + 2:w + 3], wkb_[:, w + 1:w + 2],
                                                 1.0 / 2048.0, ALU.mult)
            yield lambda: v.tensor_scalar(out=wkb_[:, w + 3:w + 4], in0=wkb_[:, w + 2:w + 3],
                                          scalar1=wkb_[:, w + 2:w + 3], scalar2=-1.0,
                                          op0=ALU.mult, op1=ALU.mult)
            yield lambda: v.scalar_tensor_tensor(out=wkb_[:, w + 4:w + 5],
                                                 in0=wkb_[:, w + 0:w + 1],
                                                 scalar=1.0 / 2048.0,
                                                 in1=wkb_[:, w + 3:w + 4],
                                                 op0=ALU.mult, op1=ALU.add)

        def spacer():
            return v.tensor_copy(vspc[:, 0:1], epsc[:, 0:1])

        for tp in range(0, NTT, 2):
            if tp >= 2:
                for k in (2 * (tp - 2), 2 * (tp - 2) + 1, 2 * (tp - 1), 2 * (tp - 1) + 1):
                    h_evac(k)
                v.wait_ge(S.s_tr, 2 * tp)
            v.wait_ge(S.sd_x, 16 * (tp + 2))
            v.wait_ge(S.sa_sqa, tp + 2)
            ca = ln_pre(tp, work1, xt[0][:, :], hbf[0][:, :])
            cb = ln_pre(tp + 1, work1, xt[1][:, :], hbf[1][:, :])
            for a, b in zip(ca, cb):
                a()
                b()
            spacer().then_inc(S.s_ld1, 2)
            v.wait_ge(S.sa_r1, tp + 2)
            for t in (tp, tp + 1):
                w = t * 6
                v.tensor_scalar(out=nmr1[:, t:t + 1], in0=work1[:, w + 2:w + 3],
                                scalar1=rstd1[:, t:t + 1], scalar2=-1.0,
                                op0=ALU.mult, op1=ALU.mult)
            for t in (tp, tp + 1):
                v.tensor_scalar_add(rstd1p[:, t:t + 1], rstd1[:, t:t + 1], 1.0)
            spacer().then_inc(S.s_ld2, 2)
        for k in range(2 * (NTT - 2), 2 * NTT):
            h_evac(k)
        for ph, dstc in ((0, bst), (1, ast)):
            for oc in range(16):
                v.wait_ge(S.s_stp, ph * 16 + oc + 1)
                par = oc % 2
                v.reduce_max(dstc[0:1, oc:oc + 1],
                             pa[0:1, par * 2048:par * 2048 + 2048],
                             axis=AX.X).then_inc(S.s_std, 1)

        for hf in range(2):
            if hf == 0:
                v.wait_ge(S.sd_sti, 64)
                v.tensor_tensor(bmax[0:1, :], stld[0:1, 0:16], stld[0:1, 16:32], ALU.max)
                v.tensor_tensor(ncw[0:1, 48:64], stld[0:1, 32:48], stld[0:1, 48:64], ALU.max)
                spacer()
                v.tensor_tensor(bmax[0:1, :], bmax[0:1, :], ncw[0:1, 48:64], ALU.max)
                v.wait_ge(S.s_std, 32)
                spacer()
                v.tensor_tensor(ncw[0:1, 0:16], ast[0:1, :], bmax[0:1, :], ALU.mult)
                spacer().then_inc(S.s_nc, 1)
                v.wait_ge(S.s_nc, 2)
                v.tensor_single_scalar(ncw[0:1, 32:48], ncw[0:1, 16:32], -SCALE, ALU.mult)
                spacer()
                v.tensor_copy(out=ncbf[0:1, :], in_=ncw[0:1, 32:48])
                spacer().then_inc(S.s_nc, 1)
                v.wait_ge(S.s_nc, 4)
                v.tensor_copy(out=negc[:, :], in_=pa[:, AVO:AVO + 16])
                spacer().then_inc(S.s_nc, 1)
            for h in range(16):
                hi = hf * 16 + h
                if h == 0:
                    v.wait_ge(SD_V[hi % 2], 64 * (hi // 2 + 1))
                    v.memset(vsb[hi % 2][:, :, 128:129], 1.0).then_inc(S.s_vrdy, 1)
                for nb in range(2):
                    nbi = hi * 2 + nb
                    v.wait_ge(S.s_av, NG * (nbi + 1))
                    if h == 0 and nb == 0:
                        v.wait_ge(S.sa_xn, 8 * (hf + 1))
                    for qs in range(4):
                        av = pa[:, AV_QS[qs]:AV_QS[qs] + 129]
                        v.tensor_scalar_max(ew[:, 2 * qs:2 * qs + 1],
                                            av[:, 128:129], RSFLOOR)
                    for qs in range(4):
                        v.reciprocal(ew[:, 2 * qs + 1:2 * qs + 2],
                                     ew[:, 2 * qs:2 * qs + 1])
                    for qs in range(4):
                        tl = nb * 4 + qs
                        av = pa[:, AV_QS[qs]:AV_QS[qs] + 129]
                        xsl = xnew[:, tl * 2048 + h * 128: tl * 2048 + (h + 1) * 128]
                        if BASE_ONLY:
                            ins = v.tensor_scalar_mul(dbgdump[:, :], av[:, 0:128],
                                                      ew[:, 2 * qs + 1:2 * qs + 2])
                        elif ATT_ONLY:
                            ins = v.tensor_scalar_mul(xsl, av[:, 0:128],
                                                      ew[:, 2 * qs + 1:2 * qs + 2])
                        else:
                            ins = v.scalar_tensor_tensor(out=xsl, in0=av[:, 0:128],
                                                         scalar=ew[:, 2 * qs + 1:2 * qs + 2],
                                                         in1=xsl,
                                                         op0=ALU.mult, op1=ALU.add)
                        if qs == 3:
                            ins.then_inc(S.s_ep, 1)
                    if nb == 0 and h < 15:
                        v.wait_ge(SD_V[(hi + 1) % 2], 64 * ((hi + 1) // 2 + 1))
                        v.memset(vsb[(hi + 1) % 2][:, :, 128:129], 1.0).then_inc(S.s_vrdy, 1)
            v.wait_ge(S.s_ep, 32 * (hf + 1))
            for tp in range(0, 8, 2):
                v.wait_ge(S.sa_sqa, 16 + hf * 8 + tp + 2)
                ca = ln_pre(tp, work2, xnew[:, tp * 2048:(tp + 1) * 2048], ln2scr[:, :])
                cb = ln_pre(tp + 1, work2, xnew[:, (tp + 1) * 2048:(tp + 2) * 2048],
                            ln2scr[:, :])
                for a, b in zip(ca, cb):
                    a()
                    b()
                spacer().then_inc(S.s_ld1, 2)
                v.wait_ge(S.sa_r1, 16 + hf * 8 + tp + 2)
                for t in (tp, tp + 1):
                    w = t * 6
                    v.tensor_scalar(out=nmr2[:, t:t + 1], in0=work2[:, w + 2:w + 3],
                                    scalar1=rstd2[:, t:t + 1], scalar2=-1.0,
                                    op0=ALU.mult, op1=ALU.mult)
                spacer().then_inc(S.s_ld2, 2)
            h2T3 = h2T.ap().rearrange("p (c t) -> p c t", c=16)
            for tl in range(8):
                for g2 in range(2):
                    k = 32 + hf * 16 + 2 * tl + g2
                    v.wait_ge(S.s_tr, k + 1)
                    src = pab[:, 6 * 1024 + (k % 2) * 1024: 6 * 1024 + (k % 2) * 1024 + 1024]
                    src = src.rearrange("p (c t) -> p c t", c=8)
                    dst = h2T3[:, g2 * 8:(g2 + 1) * 8, tl * 128:(tl + 1) * 128]
                    v.tensor_copy(out=dst, in_=src).then_inc(S.sv_t, 1)
            for jc in range(JC):
                jcg = hf * JC + jc
                v.wait_ge(S.sa_sil, jcg + 1)
                v.wait_ge(S.s_gu, 2 * jcg + 2)
                if jc >= 4:
                    v.wait_ge(S.s_dnm, hf * 512 + 32 * (jc // 4))
                v.tensor_tensor(gg[jc % 4][:, :], stmp[:, :], pa[:, 1024:2048],
                                ALU.mult).then_inc(S.s_gmu, 1)
                if jc % 4 == 3:
                    grp = jc // 4
                    for tt in range(8):
                        for ds in range(4):
                            kd = kd_idx(hf, grp, tt, ds)
                            v.wait_ge(S.s_dnm, kd + 1)
                            dn = pa[:, 2048 + (kd % 2) * 512: 2048 + (kd % 2) * 512 + 512]
                            xsl = xnew[:, tt * 2048 + ds * 512: tt * 2048 + (ds + 1) * 512]
                            v.tensor_tensor(xsl, dn, xsl, ALU.add).then_inc(S.s_dnv, 1)

    def em_tensor(t):
        for tt in range(NTT):
            for g2 in range(2):
                k = 2 * tt + g2
                if k == 0:
                    t.wait_ge(S.s_init, 17)
                t.wait_ge(S.sa_h, tt + 1)
                if k >= 2:
                    t.wait_ge(S.sv_t, k - 1)
                for j in range(8):
                    c = g2 * 8 + j
                    ins = t.transpose(
                        pab[:, (k % 2) * 1024 + j * 128:(k % 2) * 1024 + (j + 1) * 128],
                        hbf[tt % 2][:, c * 128:(c + 1) * 128], ident[:, :])
                    if j == 7:
                        ins.then_inc(S.s_tr, 1)
        def proj_phase(ph, wb, nsub, fw_base, pmm_base):
            for oc in range(16):
                par = oc % 2
                t.wait_ge(SD_W[oc % 2], 16 * (fw_base + oc // 2 + 1))
                if oc == 0 and ph == 0:
                    t.wait_ge(S.sv_t, 32)
                if oc == 0 and ph == 1:
                    t.wait_ge(S.sa_vev, 128)
                if oc >= 2:
                    t.wait_ge(S.s_std, ph * 16 + oc - 1)
                for fs in range(4):
                    for ic in range(16):
                        ins = t.matmul(pa[:, par * 2048 + fs * 512: par * 2048 + (fs + 1) * 512],
                                       wb[par][:, ic * 128:(ic + 1) * 128],
                                       hT[:, ic * 2048 + fs * 512: ic * 2048 + (fs + 1) * 512],
                                       start=(ic == 0), stop=(ic == 15))
                        if fs == 3 and ic == 15:
                            ins.then_inc(S.s_pmm, 1)
                if oc >= 1:
                    so = oc - 1
                    t.wait_ge(S.sa_sq, ph * 16 + so + 1)
                    sp = so % 2
                    for fs in range(4):
                        ins = t.matmul(pa[0:1, sp * 2048 + fs * 512: sp * 2048 + fs * 512 + 512],
                                       ones_c[:, 0:1], sqscr[:, fs * 512:(fs + 1) * 512],
                                       start=True, stop=True)
                        if fs == 3:
                            ins.then_inc(S.s_stp, 1)
            so = 15
            t.wait_ge(S.sa_sq, ph * 16 + so + 1)
            sp = so % 2
            for fs in range(4):
                ins = t.matmul(pa[0:1, sp * 2048 + fs * 512: sp * 2048 + fs * 512 + 512],
                               ones_c[:, 0:1], sqscr[:, fs * 512:(fs + 1) * 512],
                               start=True, stop=True)
                if fs == 3:
                    ins.then_inc(S.s_stp, 1)

        proj_phase(0, wkb, None, 0, 0)
        for dvs in range(8):
            t.wait_ge(SD_W[dvs % 2], 16 * (8 + dvs // 2 + 1))
            if dvs == 0:
                t.wait_ge(S.s_std, 16)
            for tt in range(16):
                kv = dvs * 16 + tt
                if kv >= 2:
                    t.wait_ge(S.sa_vev, kv - 1)
                pb = (kv % 2) * 512
                for ic in range(16):
                    t.matmul(pa[:, pb:pb + 256],
                             hT[:, ic * 2048 + tt * 128: ic * 2048 + (tt + 1) * 128],
                             wvb[dvs % 2][:, ic * 256:(ic + 1) * 256],
                             start=(ic == 0), stop=False)
                t.matmul(pa[:, pb:pb + 256], ones_r[0:1, :],
                         bv_bf[0:1, dvs * 256:(dvs + 1) * 256],
                         start=False, stop=True).then_inc(S.s_pmm, 1)
        proj_phase(1, wqb, None, 12, 144)

        for hf in range(2):
            for h in range(16):
                hi = hf * 16 + h
                par = hi % 2
                if hi == 16:
                    t.wait_ge(S.sa_sil, 64)
                    t.wait_ge(S.s_gmu, 64)
                    t.wait_ge(S.s_dnv, 512)
                if hi == 0:
                    t.wait_ge(S.s_nc, 3)
                    t.matmul(pa[:, AVO:AVO + 16], ones_r[0:1, :], ncbf[0:1, :],
                             start=True, stop=True).then_inc(S.s_nc, 1)
                t.wait_ge(SD_K[hi % 2], 64 * (hi // 2 + 1))
                t.wait_ge(SD_Q[hi % 2], 16 * (hi // 2 + 1))
                for nb in range(2):
                    nbi = hi * 2 + nb
                    qrhs = qsb[par][:, nb * 512:(nb + 1) * 512]

                    def sc(g):
                        base = (g % 2) * 1536
                        for j, ch in enumerate(GROUPS[g]):
                            ins = t.matmul(pa[:, base + j * 512: base + (j + 1) * 512],
                                           ksb[par][:, ch * 128:(ch + 1) * 128],
                                           qrhs, start=True, stop=True)
                            if j == len(GROUPS[g]) - 1:
                                ins.then_inc(S.s_sc, 1)

                    def av(g):
                        t.wait_ge(S.s_exp, gg_idx(hf, h, nb, g) + 1)
                        if g == 0:
                            t.wait_ge(S.s_ep, nbi)
                            t.wait_ge(S.s_vrdy, hi + 1)
                            if nbi == 0:
                                t.wait_ge(S.s_nc, 5)
                            if nbi == 32:
                                t.wait_ge(S.sv_t, 48)
                        base = (gg_idx(hf, h, nb, g) % 2)
                        for j, ch in enumerate(GROUPS[g]):
                            for qs in range(4):
                                ins = t.matmul(pa[:, AV_QS[qs]:AV_QS[qs] + 129],
                                               psl[base][:, j * 512 + qs * 128: j * 512 + (qs + 1) * 128],
                                               vsb[par][:, ch, :],
                                               start=(ch == 0 and qs in (0, 2)),
                                               stop=(ch == 63),
                                               skip_group_check=True)
                                if j == len(GROUPS[g]) - 1 and qs == 3:
                                    ins.then_inc(S.s_av, 1)

                    sc(0)
                    for k in range(NG - 1):
                        sc(k + 1)
                        av(k)
                    av(NG - 1)
            for tl in range(8):
                for g2 in range(2):
                    k = 32 + hf * 16 + 2 * tl + g2
                    if tl == 0 and g2 == 0:
                        t.wait_ge(S.s_ep, 32 * (hf + 1))
                    t.wait_ge(S.sa_h, 17 + hf * 8 + tl)
                    if not (tl == 0 and g2 == 0):
                        t.wait_ge(S.sv_t, k - 1)
                    for j in range(8):
                        c = g2 * 8 + j
                        ins = t.transpose(
                            pab[:, 6 * 1024 + (k % 2) * 1024 + j * 128:
                                6 * 1024 + (k % 2) * 1024 + (j + 1) * 128],
                            h2bf[tl % 2][:, c * 128:(c + 1) * 128], ident[:, :])
                        if j == 7:
                            ins.then_inc(S.s_tr, 1)
            for jc in range(JC):
                jcg = hf * JC + jc
                t.wait_ge(SD_WG[jc % 2], 16 * (hf * 32 + jc // 2 + 1))
                t.wait_ge(S.sa_sil, jcg)
                if jc == 0:
                    t.wait_ge(S.sv_t, 32 + 16 * (hf + 1))
                for ts in range(2):
                    for ic in range(16):
                        ins = t.matmul(pa[:, ts * 512:(ts + 1) * 512],
                                       wgb[jc % 2][:, ic * 128:(ic + 1) * 128],
                                       h2T[:, ic * 1024 + ts * 512: ic * 1024 + (ts + 1) * 512],
                                       start=(ic == 0), stop=(ic == 15))
                        if ts == 1 and ic == 15:
                            ins.then_inc(S.s_gu, 1)
                t.wait_ge(SD_WU[jc % 2], 16 * (hf * 32 + jc // 2 + 1))
                t.wait_ge(S.s_gmu, jcg)
                for ts in range(2):
                    for ic in range(16):
                        ins = t.matmul(pa[:, 1024 + ts * 512: 1024 + (ts + 1) * 512],
                                       wub[jc % 2][:, ic * 128:(ic + 1) * 128],
                                       h2T[:, ic * 1024 + ts * 512: ic * 1024 + (ts + 1) * 512],
                                       start=(ic == 0), stop=(ic == 15))
                        if ts == 1 and ic == 15:
                            ins.then_inc(S.s_gu, 1)
                if jc % 4 == 3:
                    grp = jc // 4
                    t.wait_ge(S.sd_wd, 16 * (hf * JC + 4 * grp + 4))
                    t.wait_ge(S.s_gmu, hf * JC + 4 * grp + 4)
                    for tt in range(8):
                        for ds in range(4):
                            kd = kd_idx(hf, grp, tt, ds)
                            if kd >= 2:
                                t.wait_ge(S.s_dnv, kd - 1)
                            for jl in range(4):
                                ins = t.matmul(
                                    pa[:, 2048 + (kd % 2) * 512: 2048 + (kd % 2) * 512 + 512],
                                    gg[jl][:, tt * 128:(tt + 1) * 128],
                                    wdg[jl][:, ds * 512:(ds + 1) * 512],
                                    start=(jl == 0), stop=(jl == 3))
                                if jl == 3:
                                    ins.then_inc(S.s_dnm, 1)

    def em_scalar(s):
        for tp in range(0, NTT, 2):
            s.wait_ge(S.sd_x, 16 * (tp + 2))
            for t in (tp, tp + 1):
                w = t * 6
                s.activation(qstg[t % 2][:, :], xt[t % 2][:, :], AF.Identity,
                             accum_out=work1[:, w + 1:w + 2]).then_inc(S.sa_sqa, 1)
            s.wait_ge(S.s_ld1, tp + 2)
            for t in (tp, tp + 1):
                w = t * 6
                s.activation(work1[:, w + 5:w + 6], work1[:, w + 4:w + 5], AF.Ln,
                             bias=epsc[:, 0:1])
                s.activation(rstd1[:, t:t + 1], work1[:, w + 5:w + 6], AF.Exp,
                             scale=-0.5).then_inc(S.sa_r1, 1)
            s.wait_ge(S.s_ld2, tp + 2)
            for t in (tp, tp + 1):
                s.activation(hbf[t % 2][:, :], xt[t % 2][:, :], AF.Identity,
                             bias=nmr1[:, t:t + 1],
                             scale=rstd1[:, t:t + 1]).then_inc(S.sa_h, 1)
        def ev_phase(ph, stg, b_sb, spill_sems):
            for oc in range(16):
                par = oc % 2
                s.wait_ge(S.s_pmm, (0 if ph == 0 else 144) + oc + 1)
                if oc >= 2:
                    s.wait_ge(spill_sems[oc % 2], 16 * (oc // 2))
                if oc == 0 and ph == 0:
                    s.wait_ge(S.sd_sti, 32)
                s.activation(stg[par][:, :], pa[:, par * 2048: par * 2048 + 2048],
                             AF.Identity, bias=b_sb[:, oc:oc + 1]).then_inc(S.sa_ev, 1)
                if oc >= 1:
                    s.wait_ge(S.s_stp, ph * 16 + oc)
                s.activation(sqscr[:, :], pa[:, par * 2048: par * 2048 + 2048],
                             AF.Square, bias=b_sb[:, oc:oc + 1]).then_inc(S.sa_sq, 1)
        ev_phase(0, kstg, bk_sb, SD_KSP)
        for dvs in range(8):
            for tt in range(16):
                kv = dvs * 16 + tt
                s.wait_ge(S.s_pmm, 16 + kv + 1)
                if kv >= 4:
                    s.wait_ge(SD_VSP[kv % 4], 16 * (kv // 4))
                pb = (kv % 2) * 512
                s.activation(vstg[kv % 4][:, :], pa[:, pb:pb + 256],
                             AF.Identity).then_inc(S.sa_vev, 1)
        ev_phase(1, qstg, bq_sb, SD_QSP)
        for hf in range(2):
            s.wait_ge(S.sd_x, 256 + 128 * (hf + 1))
            for tl in range(8):
                tt = hf * 8 + tl
                xsl = xnew[:, tl * 2048:(tl + 1) * 2048]
                s.activation(xsl, xsl, AF.Identity,
                             scale=rstd1p[:, tt:tt + 1],
                             bias=nmr1[:, tt:tt + 1]).then_inc(S.sa_xn, 1)
            if hf == 0:
                s.wait_ge(S.s_nc, 1)
                s.activation(ncw[0:1, 16:32], ncw[0:1, 0:16], AF.Ln)
                s.activation(ncw[0:1, 16:32], ncw[0:1, 16:32], AF.Exp,
                             scale=0.5).then_inc(S.s_nc, 1)
            for h in range(16):
                for nb in range(2):
                    for g in range(NG):
                        gid = gg_idx(hf, h, nb, g)
                        s.wait_ge(S.s_sc, gid + 1)
                        if gid >= 2:
                            s.wait_ge(S.s_av, gid - 1)
                        if gid == 0:
                            s.wait_ge(S.s_nc, 5)
                        cols = len(GROUPS[g]) * 512
                        base = (g % 2) * 1536
                        s.activation(psl[gid % 2][:, 0:cols], pa[:, base:base + cols],
                                     AF.Exp, bias=negc[:, h:h + 1],
                                     scale=SCALE).then_inc(S.s_exp, 1)
            s.wait_ge(S.s_ep, 32 * (hf + 1))
            for tp in range(0, 8, 2):
                for t in (tp, tp + 1):
                    w = t * 6
                    s.activation(h2bf[t % 2][:, :], xnew[:, t * 2048:(t + 1) * 2048],
                                 AF.Identity,
                                 accum_out=work2[:, w + 1:w + 2]).then_inc(S.sa_sqa, 1)
            for tp in range(0, 8, 2):
                s.wait_ge(S.s_ld1, 16 + hf * 8 + tp + 2)
                for t in (tp, tp + 1):
                    w = t * 6
                    s.activation(work2[:, w + 5:w + 6], work2[:, w + 4:w + 5], AF.Ln,
                                 bias=epsc[:, 0:1])
                    s.activation(rstd2[:, t:t + 1], work2[:, w + 5:w + 6], AF.Exp,
                                 scale=-0.5).then_inc(S.sa_r1, 1)
            for tl in range(8):
                s.wait_ge(S.s_ld2, 16 + hf * 8 + (tl - tl % 2) + 2)
                if tl >= 2:
                    s.wait_ge(S.s_tr, 32 + hf * 16 + 2 * (tl - 1))
                xsl = xnew[:, tl * 2048:(tl + 1) * 2048]
                s.activation(h2bf[tl % 2][:, :], xsl, AF.Identity,
                             scale=rstd2[:, tl:tl + 1],
                             bias=nmr2[:, tl:tl + 1]).then_inc(S.sa_h, 1)
            for jc in range(JC):
                jcg = hf * JC + jc
                s.wait_ge(S.s_gu, 2 * jcg + 1)
                s.wait_ge(S.s_gmu, jcg)
                s.activation(stmp[:, :], pa[:, 0:1024], AF.Silu).then_inc(S.sa_sil, 1)

    with nc.Block() as block:
        block.sync(em_sync)
        block.gpsimd(em_gpsimd)
        block.vector(em_vector)
        block.tensor(em_tensor)
        block.scalar(em_scalar)

    return nc


def _stage_inputs(inputs):
    x = np.asarray(inputs["L0_attn_x"], dtype=np.float32)
    Wq = np.asarray(inputs["L0_attn_Wq"], dtype=np.float32)
    Wk = np.asarray(inputs["L0_attn_Wk"], dtype=np.float32)
    Wv = np.asarray(inputs["L0_attn_Wv"], dtype=np.float32)
    bq = np.asarray(inputs["L0_attn_bq"], dtype=np.float32)
    bk = np.asarray(inputs["L0_attn_bk"], dtype=np.float32)
    bv = np.asarray(inputs["L0_attn_bv"], dtype=np.float32)
    wg = np.asarray(inputs["L0_ffn_w_gate"], dtype=np.float32)
    wu = np.asarray(inputs["L0_ffn_w_up"], dtype=np.float32)
    wd = np.asarray(inputs["L0_ffn_w_down"], dtype=np.float32)

    wq_t = np.ascontiguousarray(Wq.reshape(16, 128, 16, 128).transpose(2, 1, 0, 3)
                                ).reshape(16, 128, 2048)
    wk_t = np.ascontiguousarray(Wk.reshape(16, 128, 16, 128).transpose(2, 1, 0, 3)
                                ).reshape(16, 128, 2048)
    wv_t = np.ascontiguousarray(Wv.reshape(16, 128, 8, 256).transpose(2, 1, 0, 3)
                                ).reshape(8, 128, 4096)
    wg_t = np.ascontiguousarray(wg.reshape(64, 128, 16, 128).transpose(0, 3, 2, 1)
                                ).reshape(64, 128, 2048)
    wu_t = np.ascontiguousarray(wu.reshape(64, 128, 16, 128).transpose(0, 3, 2, 1)
                                ).reshape(64, 128, 2048)
    wd_t = np.ascontiguousarray(wd.T).reshape(64, 128, 2048)
    bq_h = np.ascontiguousarray(bq.reshape(16, 128).T)
    bk_h = np.ascontiguousarray(bk.reshape(16, 128).T)
    bv_h = np.ascontiguousarray(bv.reshape(1, 2048))

    shared = dict(wq_t=wq_t, wk_t=wk_t, wv_t=wv_t, bq_h=bq_h, bk_h=bk_h, bv_h=bv_h,
                  wg_t=wg_t, wu_t=wu_t, wd_t=wd_t)
    in_maps = []
    for c in range(8):
        b, q = c // 4, c % 4
        xs = np.ascontiguousarray(x[b, q * T:(q + 1) * T, :])
        m = {"x": xs}
        m.update(shared)
        in_maps.append(m)
    return in_maps


def kernel(**inputs):
    if "fused" not in _NC_CACHE:
        _NC_CACHE["fused"] = build_program(debug=False)
    nc = _NC_CACHE["fused"]
    in_maps = _stage_inputs(inputs)
    res = run_bass_kernel_spmd(nc, in_maps, core_ids=list(range(8)))
    out = np.empty((2, 8192, 2048), dtype=np.float32)
    for c in range(8):
        b, q = c // 4, c % 4
        out[b, q * T:(q + 1) * T, :] = res.results[c]["out"]
    return out
